# version 05
# baseline (speedup 1.0000x reference)
import contextlib
import numpy as np
import ml_dtypes

import concourse.bass as bass
import concourse.mybir as mybir
from concourse.bass_utils import run_bass_kernel_spmd

F32 = mybir.dt.float32
BF16 = mybir.dt.bfloat16
AF = mybir.ActivationFunctionType
ALU = mybir.AluOpType
AXL = mybir.AxisListType
bfdt = ml_dtypes.bfloat16

N8 = 8192
N4 = 4096
L = 2048
NB8 = 4224
NB4 = 2176
NCORE = 8
BPC = 4
EPS = 1e-5

C_EPS = 0
C_INV1024 = 1
C_NEG1 = 2
C_ZERO = 3


def _make_tables():
    t = {}
    f = np.arange(NB8)
    j = np.arange(1, 65)
    th = 2 * np.pi * np.outer(j, f) / N8
    T8re = np.zeros((65, NB8), np.float32)
    T8im = np.zeros((65, NB8), np.float32)
    T8re[:64] = np.cos(th)
    T8im[:64] = -np.sin(th)
    T8re[64, :] = 1.0
    T8re[:64, 4097:] = 0.0
    T8im[:64, 4097:] = 0.0
    t["T8re"] = T8re.astype(bfdt)
    t["T8im"] = T8im.astype(bfdt)

    fr = np.arange(NB8)[:, None].astype(np.float64)
    n = np.arange(L)[None, :].astype(np.float64)
    w8 = np.where((fr == 0) | (fr == 4096), 1.0, 2.0) / N8
    w8[fr > 4096] = 0.0
    th8 = 2 * np.pi * fr * n / N8
    t["iD8re"] = (w8 * np.cos(th8)).astype(bfdt)
    t["iD8im"] = (-w8 * np.sin(th8)).astype(bfdt)

    ff = np.arange(NB4).astype(np.float64)
    tt = np.arange(L)[:, None].astype(np.float64)
    th4 = 2 * np.pi * tt * ff[None, :] / N4
    cosm = np.cos(th4)
    sinm = -np.sin(th4)
    cosm[:, 2049:] = 0.0
    sinm[:, 2049:] = 0.0
    CS4 = np.zeros((L, 2 * NB4), np.float32)
    for fc in range(34):
        CS4[:, 128 * fc:128 * fc + 64] = cosm[:, 64 * fc:64 * fc + 64]
        CS4[:, 128 * fc + 64:128 * fc + 128] = sinm[:, 64 * fc:64 * fc + 64]
    t["CS4"] = CS4.astype(bfdt)

    th4T = 2 * np.pi * ff[:, None] * np.arange(L)[None, :].astype(np.float64) / N4
    icr = np.cos(th4T)
    ici = -np.sin(th4T)
    icr[2049:, :] = 0.0
    ici[2049:, :] = 0.0
    t["iCSre"] = icr.astype(bfdt)
    t["iCSim"] = ici.astype(bfdt)

    wf4 = np.where((ff == 0) | (ff == 2048), 1.0, 2.0) / N4
    wf4[ff > 2048] = 0.0
    t["wf4"] = np.ascontiguousarray(wf4.reshape(17, 128).T).astype(np.float32)
    return t


def _make_bfly(up_w, dn_w):
    D = 4096
    def scatter(w, transpose):
        out = np.zeros((7, 128, D), np.float32)
        for s in range(7):
            sig = 2 ** s
            jl = np.arange(D)
            m = (jl & sig) == 0
            jls = jl[m]
            jhs = jls + sig
            ks = (jls // (2 * sig)) * sig + (jls % sig)
            c = jls // 128
            il = jls % 128
            ih = jhs % 128
            w00 = w[s, ks, 0, 0]; w01 = w[s, ks, 0, 1]
            w10 = w[s, ks, 1, 0]; w11 = w[s, ks, 1, 1]
            if transpose:
                w01, w10 = w10, w01
            out[s, il, c * 128 + il] = w00
            out[s, ih, c * 128 + il] = w01
            out[s, il, c * 128 + ih] = w10
            out[s, ih, c * 128 + ih] = w11
        return out
    def high(w, transpose):
        out = np.zeros((5, 4, D), np.float32)
        for si, s in enumerate(range(7, 12)):
            sig = 2 ** s
            jl = np.arange(D)
            m = (jl & sig) == 0
            jls = jl[m]
            jhs = jls + sig
            ks = (jls // (2 * sig)) * sig + (jls % sig)
            w00 = w[s, ks, 0, 0]; w01 = w[s, ks, 0, 1]
            w10 = w[s, ks, 1, 0]; w11 = w[s, ks, 1, 1]
            if transpose:
                w01, w10 = w10, w01
            out[si, 0, jls] = w00
            out[si, 1, jls] = w01
            out[si, 2, jhs] = w10
            out[si, 3, jhs] = w11
        return np.ascontiguousarray(
            out.reshape(20, 32, 128).transpose(0, 2, 1)).astype(np.float32)
    return dict(up_low=scatter(up_w, False), up_high=high(up_w, False),
                dn_low=scatter(dn_w, True), dn_high=high(dn_w, True))


class Sem:
    __slots__ = ("h", "c", "name")
    def __init__(self, h, name):
        self.h = h
        self.c = 0
        self.name = name


class Builder:
    def __init__(self):
        self.ops = []

    def emit(self, eng, fn, waits=(), incs=()):
        self.ops.append((eng, fn, tuple(waits), tuple(incs)))

    def check_deadlock(self):
        streams = {k: [o for o in self.ops if o[0] == k] for k in "TVSGY"}
        idx = {k: 0 for k in "TVSGY"}
        counts = {}
        total = sum(len(v) for v in streams.values())
        done = 0
        progress = True
        while progress:
            progress = False
            for k in "TVSGY":
                while idx[k] < len(streams[k]):
                    _, fn, waits, incs = streams[k][idx[k]]
                    if all(counts.get(s, 0) >= v for s, v in waits):
                        for s, n in incs:
                            counts[s] = counts.get(s, 0) + n
                        idx[k] += 1
                        done += 1
                        progress = True
                    else:
                        break
        if done != total:
            msgs = []
            for k in "TVSGY":
                if idx[k] < len(streams[k]):
                    _, fn, waits, incs = streams[k][idx[k]]
                    pend = [(s, v, counts.get(s, 0)) for s, v in waits if counts.get(s, 0) < v]
                    msgs.append(f"{k}@{idx[k]}/{len(streams[k])} needs {pend}")
            raise RuntimeError("DEADLOCK: " + "; ".join(msgs))


def build_program(debug=False):
    nc = bass.Bass("TRN2", target_bir_lowering=False, debug=False)

    def din(name, shape, dt=F32):
        return nc.dram_tensor(name, list(shape), dt, kind="ExternalInput").ap()
    def dout(name, shape, dt=F32):
        return nc.dram_tensor(name, list(shape), dt, kind="ExternalOutput").ap()
    def dtmp(name, shape, dt=F32):
        return nc.dram_tensor(name, list(shape), dt).ap()

    u4 = din("u4", [BPC, L, 1024])
    aTx = din("aTx", [65, 128])
    bTx = din("bTx", [65, 128])
    T8re = din("T8re", [65, NB8], BF16)
    T8im = din("T8im", [65, NB8], BF16)
    iD8re = din("iD8re", [NB8, L], BF16)
    iD8im = din("iD8im", [NB8, L], BF16)
    CS4 = din("CS4", [L, 2 * NB4], BF16)
    iCSre = din("iCSre", [NB4, L], BF16)
    iCSim = din("iCSim", [NB4, L], BF16)
    wf4 = din("wf4", [128, 17])
    h0rep = din("h0rep", [128, 1024])
    bconrep = din("bconrep", [128, 1024])
    dbrep = din("dbrep", [128, 1024])
    upb_fm = din("upb_fm", [128, 32])
    bexp_fm = din("bexp_fm", [128, 32])
    Wexp_sl = din("Wexp_sl", [128, 4096])
    Wcon_sl = din("Wcon_sl", [4096, 128])
    bflowu = din("bflowu", [7, 128, 4096])
    bflowd = din("bflowd", [7, 128, 4096])
    bfhighu = din("bfhighu", [20, 128, 32])
    bfhighd = din("bfhighd", [20, 128, 32])
    ident = din("ident", [128, 128], BF16)
    consts = din("consts", [128, 8])
    fsrc = din("fsrc", [128, 4])

    out = dout("out", [BPC, L, 1024])
    x1d = dtmp("x1d", [BPC * L, 1024])
    if debug:
        d_kpk = dout("d_kpk", [2, 17, 128, 1024], BF16)
        d_biasp = dout("d_biasp", [128, 32])
        d_w2 = dout("d_w2", [128, 4096], BF16)
        d_wh = dout("d_wh", [4096, 128], BF16)
        d_ksl = dout("d_ksl", [128, 2048], BF16)
        d_ure0 = dout("d_ure0", [128, 17, 1024], BF16)
        d_uim0 = dout("d_uim0", [128, 17, 1024], BF16)
        d_yre0 = dout("d_yre0", [128, 17, 1024], BF16)
        d_yim0 = dout("d_yim0", [128, 17, 1024], BF16)
        d_xh0 = dout("d_xh0", [128, 16, 1024], BF16)
        d_x1 = dout("d_x1", [BPC * L, 1024])
        d_stat = dout("d_stat", [128, 10, 64])
        d_uki = dout("d_uki", [128, 34, 128], BF16)
        d_khs = dout("d_khs", [128, 2, 17, 128], BF16)
    kcon = dtmp("kcon", [2, 17, 128, 128], BF16)
    kag = dtmp("kag", [16, 17, 128, 128], BF16)
    kpk = dtmp("kpk", [2, 17, 128, 1024], BF16)
    w2con = dtmp("w2con", [128, 4096], BF16)
    w2ag = dtmp("w2ag", [1024, 4096], BF16)
    whcon = dtmp("whcon", [4096, 128], BF16)
    whag = dtmp("whag", [NCORE * 4096, 128], BF16)

    stack = contextlib.ExitStack()
    def sball(name, shape, dt=F32):
        return stack.enter_context(nc.sbuf_tensor(name, list(shape), dt))

    h0s = sball("h0s", [128, 1024])
    bias3 = sball("bias3", [128, 1024])
    biasp = sball("biasp", [128, 32])
    idents = sball("idents", [128, 128], BF16)
    aTs = sball("aTs", [65, 128], BF16)
    bTs = sball("bTs", [65, 128], BF16)
    wfs = sball("wfs", [128, 17])
    cst = sball("cst", [128, 8])
    smt = sball("smt", [128, 4])
    fencebuf = sball("fencebuf", [128, 4])

    EPS_AP = cst[:, C_EPS:C_EPS + 1]
    I1024 = cst[:, C_INV1024:C_INV1024 + 1]
    NEG1 = cst[:, C_NEG1:C_NEG1 + 1]
    ZERO = cst[:, C_ZERO:C_ZERO + 1]

    ps_a = stack.enter_context(nc.psum_tensor("ps_a", [128, 1024], F32))
    ps_b = stack.enter_context(nc.psum_tensor("ps_b", [128, 1024], F32))
    ps_c = stack.enter_context(nc.psum_tensor("ps_c", [128, 1024], F32))
    ps_t = stack.enter_context(nc.psum_tensor("ps_t", [128, 1024], BF16))

    B = Builder()
    sems = {}
    def mksem(name):
        s = Sem(stack.enter_context(nc.semaphore(name)), name)
        sems[name] = s
        return s
    sLD = mksem("sLD")
    sGD = mksem("sGD")
    sT = mksem("sT")
    sV = mksem("sV")
    sS = mksem("sS")
    sG = mksem("sG")
    sT2 = mksem("sT2")

    def W(*pairs):
        return [(s.name, v) for s, v in pairs]

    def op(eng, method, sem=None, waits=(), **kw):
        incs = []
        if sem is not None:
            sem.c += 1
            incs = [(sem.name, 1)]
        def fn(e, m=method, kk=kw, s=sem, ww=tuple(waits)):
            for sh, v in ww:
                e.wait_ge(sems[sh].h, v)
            i = getattr(e, m)(**kk)
            if s is not None:
                i.then_inc(s.h, 1)
        B.emit(eng, fn, waits=waits, incs=incs)

    def mm(outap, lhsT, rhs, start, stop, sem=None, waits=()):
        incs = []
        if sem is not None:
            sem.c += 1
            incs = [(sem.name, 1)]
        def fn(e, o=outap, l=lhsT, r=rhs, st=start, sp=stop, s=sem, ww=tuple(waits)):
            for sh, v in ww:
                e.wait_ge(sems[sh].h, v)
            i = e.matmul(o, l, r, start=st, stop=sp)
            if s is not None:
                i.then_inc(s.h, 1)
        B.emit("T", fn, waits=waits, incs=incs)

    def tr(outap, inap, sem=None, waits=()):
        incs = []
        if sem is not None:
            sem.c += 1
            incs = [(sem.name, 1)]
        def fn(e, o=outap, i=inap, idn=idents[:, :], s=sem, ww=tuple(waits)):
            for sh, v in ww:
                e.wait_ge(sems[sh].h, v)
            inst = e.transpose(o, i, idn)
            if s is not None:
                inst.then_inc(s.h, 1)
        B.emit("T", fn, waits=waits, incs=incs)

    def dma(eng, outap, inap, sem, waits=(), fence=1):
        nf = int(fence)
        sem.c += 16 * (1 + nf)
        def fn(e, o=outap, i=inap, s=sem, ww=tuple(waits), f=nf):
            for sh, v in ww:
                e.wait_ge(sems[sh].h, v)
            e.dma_start(out=o, in_=i).then_inc(s.h, 16)
            for _ in range(f):
                e.dma_start(out=fencebuf[:, :], in_=fsrc).then_inc(s.h, 16)
        B.emit(eng, fn, waits=waits, incs=[(sem.name, 16 * (1 + nf))])

    def cc(inap, outap, waits=()):
        sG.c += 1
        def fn(e, i=inap, o=outap, ww=tuple(waits)):
            for sh, v in ww:
                e.wait_ge(sems[sh].h, v)
            e.collective_compute("AllGather", ALU.bypass,
                                 replica_groups=[list(range(NCORE))],
                                 ins=[i.opt()], outs=[o.opt()]).then_inc(sG.h, 1)
        B.emit("G", fn, waits=waits, incs=[(sG.name, 1)])

    def barrier(eng, waits):
        def fn(e, ww=tuple(waits)):
            for sh, v in ww:
                e.wait_ge(sems[sh].h, v)
        B.emit(eng, fn, waits=waits, incs=())

    def sfence():
        op("S", "activation", sem=sS, out=smt[:, 0:1], in_=smt[:, 0:1],
           func=AF.Identity, bias=0.0, scale=1.0)

    def vfence():
        op("V", "tensor_copy", sem=sV, out=smt[:, 1:2], in_=smt[:, 1:2])

    def gfence():
        op("G", "tensor_copy", sem=sG, out=smt[:, 2:3], in_=smt[:, 2:3])

    def ln_finish(ssum, ssq, m_ap, var_ap, sd_ap, r_ap, mb_ap, vwaits=()):
        op("V", "tensor_scalar", waits=vwaits, out=m_ap, in0=ssum, scalar1=I1024,
           scalar2=None, op0=ALU.mult)
        op("V", "tensor_tensor", out=sd_ap, in0=m_ap, in1=m_ap, op=ALU.mult)
        op("V", "tensor_scalar", out=var_ap, in0=ssq, scalar1=I1024,
           scalar2=None, op0=ALU.mult)
        op("V", "tensor_tensor", out=var_ap, in0=var_ap, in1=sd_ap, op=ALU.subtract)
        vfence()
        op("S", "activation", waits=W((sV, sV.c)), out=sd_ap, in_=var_ap,
           func=AF.Sqrt, bias=EPS_AP, scale=1.0)
        sfence()
        op("V", "reciprocal", waits=W((sS, sS.c)), out=r_ap, in_=sd_ap)
        op("V", "tensor_tensor", out=mb_ap, in0=m_ap, in1=r_ap, op=ALU.mult)
        op("V", "tensor_scalar", out=mb_ap, in0=mb_ap, scalar1=NEG1,
           scalar2=None, op0=ALU.mult)
        vfence()

    dma("Y", h0s[:, :], h0rep, sLD)
    dma("Y", bias3[:, :], bconrep, sLD)
    dma("Y", idents[:, :], ident, sLD)
    dma("G", aTs[:, :], aTx, sGD)
    dma("G", bTs[:, :], bTx, sGD)
    dma("Y", wfs[:, :], wf4, sLD)
    dma("Y", cst[:, :], consts, sLD)

    prep = contextlib.ExitStack()
    def pbuf(name, shape, dt=F32):
        return prep.enter_context(nc.sbuf_tensor(name, list(shape), dt))

    dbt = pbuf("dbt", [128, 1024])
    upbs = pbuf("upbs", [128, 32])
    bexps = pbuf("bexps", [128, 32])
    dma("Y", dbt[:, :], dbrep, sLD)
    dma("Y", upbs[:, :], upb_fm, sLD)
    dma("Y", bexps[:, :], bexp_fm, sLD)
    op("V", "tensor_tensor", sem=sV, waits=W((sLD, sLD.c)),
       out=bias3[:, :], in0=bias3[:, :], in1=dbt[:, :], op=ALU.add)

    t8s = pbuf("t8s", [65, 2, 2, 128], BF16)
    Kre = pbuf("Kre", [128, 33, 128], BF16)
    Kim = pbuf("Kim", [128, 33, 128], BF16)
    dt1 = pbuf("dt1", [128, 128])
    dt2 = pbuf("dt2", [128, 128])
    dt3 = pbuf("dt3", [128, 128])
    kab = pbuf("kab", [128, 4, 128])

    for fc in range(33):
        slot = fc % 2
        wv = W((sV, sV.c)) if fc >= 2 else []
        dma("Y", t8s[:, slot, 0, :], T8re[:, 128 * fc:128 * fc + 128], sLD, waits=wv)
        dma("Y", t8s[:, slot, 1, :], T8im[:, 128 * fc:128 * fc + 128], sLD, fence=2)
        ps = ps_a if slot == 0 else ps_b
        mw = W((sLD, sLD.c), (sGD, sGD.c)) + (W((sV, sV.c)) if fc >= 2 else [])
        mm(ps[:, 0:128], t8s[:, slot, 0, :], aTs[0:65, :], True, True, waits=mw)
        mm(ps[:, 128:256], t8s[:, slot, 1, :], aTs[0:65, :], True, True)
        mm(ps[:, 256:384], t8s[:, slot, 0, :], bTs[0:65, :], True, True)
        mm(ps[:, 384:512], t8s[:, slot, 1, :], bTs[0:65, :], True, True, sem=sT)
        op("V", "tensor_copy", waits=W((sT, sT.c)), out=kab[:, 0, :], in_=ps[:, 0:128])
        op("V", "tensor_copy", out=kab[:, 1, :], in_=ps[:, 128:256])
        op("V", "tensor_copy", out=kab[:, 2, :], in_=ps[:, 256:384])
        op("V", "tensor_copy", out=kab[:, 3, :], in_=ps[:, 384:512])
        cAr, cAi, cBr, cBi = kab[:, 0, :], kab[:, 1, :], kab[:, 2, :], kab[:, 3, :]
        op("V", "tensor_tensor", out=dt1[:, :], in0=cAr, in1=cAr, op=ALU.mult)
        op("V", "tensor_tensor", out=dt2[:, :], in0=cAi, in1=cAi, op=ALU.mult)
        op("V", "tensor_tensor", out=dt1[:, :], in0=dt1[:, :], in1=dt2[:, :], op=ALU.add)
        op("V", "reciprocal", out=dt1[:, :], in_=dt1[:, :])
        op("V", "tensor_tensor", out=dt2[:, :], in0=cBr, in1=cAr, op=ALU.mult)
        op("V", "tensor_tensor", out=dt3[:, :], in0=cBi, in1=cAi, op=ALU.mult)
        op("V", "tensor_tensor", out=dt2[:, :], in0=dt2[:, :], in1=dt3[:, :], op=ALU.add)
        op("V", "tensor_tensor", out=Kre[:, fc, :], in0=dt2[:, :], in1=dt1[:, :], op=ALU.mult)
        op("V", "tensor_tensor", out=dt2[:, :], in0=cBi, in1=cAr, op=ALU.mult)
        op("V", "tensor_tensor", out=dt3[:, :], in0=cBr, in1=cAi, op=ALU.mult)
        op("V", "tensor_tensor", out=dt2[:, :], in0=dt2[:, :], in1=dt3[:, :], op=ALU.subtract)
        op("V", "tensor_tensor", sem=sV, out=Kim[:, fc, :], in0=dt2[:, :], in1=dt1[:, :], op=ALU.mult)

    id8s = pbuf("id8s", [128, 2, 2, 512], BF16)
    kslice = pbuf("kslice", [128, 2048], BF16)
    barrier("T", W((sV, sV.c)))
    for q in range(4):
        psq = ps_a if q % 2 == 0 else ps_b
        qo = 512 * (q // 2)
        for fc in range(33):
            slot = fc % 2
            wv = W((sT2, sT2.c)) if (fc >= 2 or q > 0) else []
            dma("Y", id8s[:, slot, 0, :], iD8re[128 * fc:128 * fc + 128, 512 * q:512 * q + 512], sLD, waits=wv)
            dma("Y", id8s[:, slot, 1, :], iD8im[128 * fc:128 * fc + 128, 512 * q:512 * q + 512], sLD, fence=2)
            mw = W((sLD, sLD.c))
            if fc == 0 and q >= 2:
                mw += W((sV, sV.c))
            mm(psq[:, qo:qo + 512], Kre[:, fc, :], id8s[:, slot, 0, :], fc == 0, False, waits=mw)
            mm(psq[:, qo:qo + 512], Kim[:, fc, :], id8s[:, slot, 1, :], False, fc == 32, sem=sT2)
        op("V", "tensor_copy", sem=sV, waits=W((sT2, sT2.c)),
           out=kslice[:, 512 * q:512 * q + 512], in_=psq[:, qo:qo + 512])

    kT = pbuf("kT", [128, 16, 128], BF16)
    for tt in range(16):
        tr(ps_t[:, 0:128], kslice[:, 128 * tt:128 * tt + 128], sem=sT, waits=W((sV, sV.c)))
        op("V", "tensor_copy", sem=sV, waits=W((sT, sT.c)), out=kT[:, tt, :], in_=ps_t[:, 0:128])

    if debug:
        dma("G", d_ksl, kslice[:, :], sGD, waits=W((sV, sV.c)))
    cs_st = pbuf("cs_st", [128, 4, 128], BF16)
    uki = pbuf("uki", [128, 34, 128], BF16)
    for fc in range(34):
        psq = ps_a if fc % 2 == 0 else ps_b
        for tc in range(16):
            i = fc * 16 + tc
            slot = i % 4
            wv = W((sT2, sT2.c - 3)) if i >= 4 else []
            dma("Y", cs_st[:, slot, :], CS4[128 * tc:128 * tc + 128, 128 * fc:128 * fc + 128], sLD, waits=wv, fence=2)
            mw = W((sLD, sLD.c))
            if tc == 0:
                mw += W((sV, sV.c))
            mm(psq[:, 0:128], cs_st[:, slot, :], kT[:, tc, :], tc == 0, tc == 15,
               sem=sT2, waits=mw)
        op("V", "tensor_copy", sem=sV, waits=W((sT2, sT2.c)), out=uki[:, fc, :], in_=psq[:, 0:128])

    khs = pbuf("khs", [128, 2, 17, 128], BF16)
    for fc in range(34):
        half = 64 * (fc % 2)
        dma("G", khs[half:half + 64, 0, fc // 2, :], uki[0:64, fc, :], sGD, waits=W((sV, sV.c)))
        dma("G", khs[half:half + 64, 1, fc // 2, :], uki[64:128, fc, :], sGD, fence=2)
    for c in range(2):
        op("V", "tensor_tensor", sem=sV, waits=W((sGD, sGD.c)),
           out=khs[:, c, :, :], in0=khs[:, c, :, :],
           in1=wfs.ap()[:, :].unsqueeze(2).broadcast_to([128, 17, 128]), op=ALU.mult)
    if debug:
        dma("G", d_uki, uki[:, :, :], sGD, waits=W((sV, sV.c)))
        dma("G", d_khs, khs[:, :, :, :], sGD)
    dma("G", kcon.rearrange("c f p ch -> p c f ch"), khs[:, :, :, :], sGD, waits=W((sV, sV.c)))
    cc(kcon, kag, waits=W((sGD, sGD.c)))
    for r in range(NCORE):
        dma("G", kpk[:, :, :, 128 * r:128 * r + 128], kag[2 * r:2 * r + 2, :, :, :], sGD,
            waits=(W((sG, sG.c)) if r == 0 else []))
    if debug:
        dma("G", d_kpk, kpk, sGD)
    KPK_DONE = sGD.c

    wbig = pbuf("wbig", [128, 4096], BF16)
    wta = pbuf("wta", [128, 32, 132], BF16)
    wtb = pbuf("wtb", [128, 32, 132], BF16)
    bfst = pbuf("bfst", [128, 2, 128], BF16)
    whw = pbuf("whw", [128, 4, 32])

    dma("G", wbig[:, :], Wexp_sl, sGD, fence=2)
    for c in range(32):
        tw = W((sGD, sGD.c)) if c == 0 else []
        tr(ps_t[:, 0:128], wbig[:, 128 * c:128 * c + 128], sem=sT, waits=tw + W((sV, sV.c)))
        op("V", "tensor_copy", sem=sV, waits=W((sT, sT.c)), out=wta[:, c, 0:128], in_=ps_t[:, 0:128])
    op("V", "tensor_copy", sem=sV, waits=W((sLD, sLD.c)), out=wta[:, :, 128], in_=bexps[:, :])

    def low_stage_pass(tbl_ap, cur, oth, s):
        for c in range(32):
            slot = c % 2
            wv = W((sT2, sT2.c)) if c >= 2 else []
            dma("G", bfst[:, slot, :], tbl_ap[s, :, 128 * c:128 * c + 128], sGD, waits=wv, fence=2)
            psh = ps_c[:, 0:132] if slot == 0 else ps_c[:, 512:644]
            mm(psh, bfst[:, slot, :], cur[:, c, 0:132], True, True, sem=sT2,
               waits=W((sGD, sGD.c), (sV, sV.c)))
            op("V", "tensor_copy", sem=sV, waits=W((sT2, sT2.c)), out=oth[:, c, 0:132], in_=psh)
        return oth, cur

    def high_stage_pass(tbl_ap, cur, oth, si):
        dma("Y", whw[:, :, :], tbl_ap[4 * si:4 * si + 4, :, :].rearrange("c p k -> p c k"), sLD,
            waits=W((sV, sV.c)), fence=2)
        sigc = 2 ** si
        for c in range(32):
            cp = c ^ sigc
            if c < cp:
                wa = whw[:, 0, c:c + 1]
                wb = whw[:, 1, c:c + 1]
            else:
                wa = whw[:, 3, c:c + 1]
                wb = whw[:, 2, c:c + 1]
            wv = W((sLD, sLD.c)) if c == 0 else []
            op("V", "tensor_scalar", waits=wv, out=oth[:, c, 0:132], in0=cur[:, c, 0:132],
               scalar1=wa, scalar2=None, op0=ALU.mult)
            op("V", "scalar_tensor_tensor", out=oth[:, c, 0:132], in0=cur[:, cp, 0:132],
               scalar=wb, in1=oth[:, c, 0:132], op0=ALU.mult, op1=ALU.add)
        op("V", "tensor_copy", sem=sV, out=dt1[:, 0:1], in_=dt2[:, 0:1])
        return oth, cur

    cur, oth = wta, wtb
    for s in range(7):
        cur, oth = low_stage_pass(bflowu, cur, oth, s)
    for si in range(5):
        cur, oth = high_stage_pass(bfhighu, cur, oth, si)
    op("V", "tensor_tensor", sem=sV, waits=W((sLD, sLD.c)),
       out=biasp[:, :], in0=cur[:, :, 128], in1=upbs[:, :], op=ALU.add)
    for c in range(32):
        tr(ps_t[:, 0:128], cur[:, c, 0:128], sem=sT, waits=W((sV, sV.c)))
        op("V", "tensor_copy", sem=sV, waits=W((sT, sT.c)),
           out=wbig[:, 128 * c:128 * c + 128], in_=ps_t[:, 0:128])
    dma("G", w2con, wbig[:, :], sGD, waits=W((sV, sV.c)))
    if debug:
        dma("G", d_biasp.unsqueeze(0) if False else d_biasp, biasp[:, :], sGD)
        dma("G", d_w2, wbig[:, :], sGD)
    cc(w2con, w2ag, waits=W((sGD, sGD.c)))

    dma("G", wta.ap()[:, :, 0:128], Wcon_sl.rearrange("(c p) o -> p c o", p=128), sGD,
        waits=W((sV, sV.c), (sT, sT.c)), fence=2)
    barrier("V", W((sGD, sGD.c)))
    cur, oth = wta, wtb
    for si in reversed(range(5)):
        cur, oth = high_stage_pass(bfhighd, cur, oth, si)
    for s in reversed(range(7)):
        cur, oth = low_stage_pass(bflowd, cur, oth, s)
    dma("G", whcon.rearrange("(c p) o -> p c o", p=128), cur[:, :, 0:128], sGD,
        waits=W((sV, sV.c), (sT2, sT2.c)))
    if debug:
        dma("G", d_wh, whcon, sGD)
    cc(whcon, whag, waits=W((sGD, sGD.c)))
    AG_ALL = sG.c

    prep.close()

    conv = contextlib.ExitStack()
    def cbuf(name, shape, dt=F32):
        return conv.enter_context(nc.sbuf_tensor(name, list(shape), dt))

    xhb = cbuf("xhb", [128, 16, 1024], BF16)
    Ure = cbuf("Ure", [128, 17, 1024], BF16)
    Uim = cbuf("Uim", [128, 17, 1024], BF16)
    ints = cbuf("ints", [128, 2, 1024], BF16)
    cstg = cbuf("cstg", [128, 4, 128], BF16)
    istg = cbuf("istg", [128, 8, 128], BF16)
    khsl = cbuf("khsl", [128, 2, 2, 1024], BF16)
    pt1 = cbuf("pt1", [128, 1024])
    pt2 = cbuf("pt2", [128, 1024])
    pt3 = cbuf("pt3", [128, 1024])
    gt1 = cbuf("gt1", [128, 1024])
    gt2 = cbuf("gt2", [128, 1024])
    gt3 = cbuf("gt3", [128, 1024])
    utl = cbuf("utl", [128, 2, 1024])
    asm1 = cbuf("asm1", [128, 1024])
    asm2 = cbuf("asm2", [128, 1024])
    stat = cbuf("stat", [128, 10, 64])
    x1t = cbuf("x1t", [128, 2, 1024])
    dbgstat = cbuf("dbgstat", [128, 10, 64]) if debug else None

    s_sum = stat.ap()[:, 0, :]
    s_sq = stat.ap()[:, 1, :]
    s_r1 = stat.ap()[:, 2, :]
    s_mb = stat.ap()[:, 3, :]
    s_m = stat.ap()[:, 4, :]
    s_var = stat.ap()[:, 5, :]
    s_sd = stat.ap()[:, 6, :]
    s_r2 = stat.ap()[:, 8, :]
    s_mb2 = stat.ap()[:, 9, :]

    barrier("G", W((sG, AG_ALL), (sGD, KPK_DONE)))
    barrier("Y", W((sG, AG_ALL), (sGD, sGD.c), (sV, sV.c), (sT2, sT2.c)))
    barrier("V", W((sG, AG_ALL), (sGD, sGD.c)))
    barrier("S", W((sG, AG_ALL), (sGD, sGD.c)))
    barrier("T", W((sG, AG_ALL), (sGD, sGD.c)))

    for b in range(BPC):
        ub = u4[b]
        for tc in range(16):
            slot = tc % 2
            wv = W((sV, sV.c)) if tc >= 2 else W((sV, sV.c), (sS, sS.c))
            dma("Y", utl[:, slot, :], ub[128 * tc:128 * tc + 128, :], sLD, waits=wv, fence=2)
            op("V", "tensor_reduce", waits=W((sLD, sLD.c)),
               out=s_sum[:, tc:tc + 1], in_=utl[:, slot, :], axis=AXL.X, op=ALU.add)
            op("V", "tensor_tensor", out=asm1[:, :], in0=utl[:, slot, :],
               in1=utl[:, slot, :], op=ALU.mult)
            op("V", "tensor_reduce", sem=sV,
               out=s_sq[:, tc:tc + 1], in_=asm1[:, :], axis=AXL.X, op=ALU.add)
        ln_finish(s_sum, s_sq, s_m, s_var, s_sd, s_r1, s_mb,
                  vwaits=W((sV, sV.c)))
        if debug and b == 0:
            op("V", "tensor_copy", sem=sV, out=dbgstat[:, :, :], in_=stat[:, :, :])
            dma("G", d_stat, dbgstat[:, :, :], sGD, waits=W((sV, sV.c)))
        for tc in range(16):
            slot = tc % 2
            wv = W((sS, sS.c)) if tc >= 2 else W((sV, sV.c))
            dma("Y", utl[:, slot, :], ub[128 * tc:128 * tc + 128, :], sLD, waits=wv, fence=2)
            sw = W((sLD, sLD.c)) + (W((sV, sV.c)) if tc == 0 else [])
            op("S", "activation", sem=sS, waits=sw, out=xhb[:, tc, :], in_=utl[:, slot, :],
               func=AF.Identity, bias=s_mb[:, tc:tc + 1], scale=s_r1[:, tc:tc + 1])
        sfence()
        XH_DONE = sS.c
        for fc in range(34):
            psq = ps_a if fc % 2 == 0 else ps_b
            for tc in range(16):
                i = fc * 16 + tc
                slot = i % 4
                wv = W((sT2, sT2.c - 3)) if i >= 4 else []
                dma("Y", cstg[:, slot, :], CS4[128 * tc:128 * tc + 128, 128 * fc:128 * fc + 128],
                    sLD, waits=wv)
                mw = W((sLD, sLD.c))
                if tc == 0:
                    mw += W((sS, XH_DONE))
                    if fc >= 2:
                        mw += W((sV, sV.c))
                mm(psq[:, 0:512], cstg[:, slot, :], xhb[:, tc, 0:512], tc == 0, tc == 15, waits=mw)
                mm(psq[:, 512:1024], cstg[:, slot, :], xhb[:, tc, 512:1024], tc == 0, tc == 15,
                   sem=sT2)
            islot = fc % 2
            vw = W((sT2, sT2.c)) + (W((sGD, sGD.c)) if fc >= 2 else [])
            op("V", "tensor_copy", waits=vw, out=ints[:, islot, 0:512], in_=psq[:, 0:512])
            op("V", "tensor_copy", sem=sV, out=ints[:, islot, 512:1024], in_=psq[:, 512:1024])
            half = 64 * (fc % 2)
            dma("G", Ure[half:half + 64, fc // 2, :], ints[0:64, islot, :], sGD, waits=W((sV, sV.c)))
            dma("G", Uim[half:half + 64, fc // 2, :], ints[64:128, islot, :], sGD, fence=2)
        if debug and b == 0:
            dma("G", d_xh0, xhb[:, :, :], sGD, waits=W((sS, XH_DONE)))
            dma("G", d_ure0, Ure[:, :, :], sGD)
            dma("G", d_uim0, Uim[:, :, :], sGD)
        barrier("V", W((sGD, sGD.c)))
        barrier("G", W((sGD, sGD.c)))
        for fcp in range(17):
            slot = fcp % 2
            qeng, t1, t2, t3, qsem = (("V", pt1, pt2, pt3, sV) if fcp % 2 == 0
                                      else ("G", gt1, gt2, gt3, sG))
            wv = W((qsem, qsem.c)) if fcp >= 2 else []
            dma("Y", khsl[:, slot, 0, :], kpk[0, fcp, :, :], sLD, waits=wv)
            dma("Y", khsl[:, slot, 1, :], kpk[1, fcp, :, :], sLD, fence=2)
            kr = khsl[:, slot, 0, :]
            ki = khsl[:, slot, 1, :]
            ur = Ure[:, fcp, :]
            ui = Uim[:, fcp, :]
            op(qeng, "tensor_tensor", waits=W((sLD, sLD.c)), out=t1[:, :], in0=ur, in1=kr, op=ALU.mult)
            op(qeng, "tensor_tensor", out=t2[:, :], in0=ui, in1=ki, op=ALU.mult)
            op(qeng, "tensor_tensor", out=t3[:, :], in0=ur, in1=ki, op=ALU.mult)
            op(qeng, "tensor_tensor", out=ur, in0=t1[:, :], in1=t2[:, :], op=ALU.subtract)
            op(qeng, "tensor_tensor", out=t1[:, :], in0=ui, in1=kr, op=ALU.mult)
            op(qeng, "tensor_tensor", sem=qsem, out=ui, in0=t3[:, :], in1=t1[:, :], op=ALU.add)
        PW_V = sV.c
        PW_G = sG.c
        if debug and b == 0:
            dma("G", d_yre0, Ure[:, :, :], sGD, waits=W((sV, PW_V), (sG, PW_G)))
            dma("G", d_yim0, Uim[:, :, :], sGD)
        for tc in range(16):
            psq = ps_a if tc % 2 == 0 else ps_b
            for fcp in range(17):
                i = tc * 17 + fcp
                slot = i % 4
                wv = W((sT2, sT2.c - 3)) if i >= 4 else []
                dma("Y", istg[:, 2 * slot, :], iCSre[128 * fcp:128 * fcp + 128, 128 * tc:128 * tc + 128],
                    sLD, waits=wv)
                dma("Y", istg[:, 2 * slot + 1, :], iCSim[128 * fcp:128 * fcp + 128, 128 * tc:128 * tc + 128],
                    sLD)
                mw = W((sLD, sLD.c))
                if i == 0:
                    mw += W((sV, PW_V), (sG, PW_G))
                if fcp == 0 and tc >= 2:
                    mw += W((sV, sV.c))
                first = fcp == 0
                last = fcp == 16
                mm(psq[:, 0:512], istg[:, 2 * slot, :], Ure[:, fcp, 0:512], first, False, waits=mw)
                mm(psq[:, 512:1024], istg[:, 2 * slot, :], Ure[:, fcp, 512:1024], first, False)
                mm(psq[:, 0:512], istg[:, 2 * slot + 1, :], Uim[:, fcp, 0:512], False, last)
                mm(psq[:, 512:1024], istg[:, 2 * slot + 1, :], Uim[:, fcp, 512:1024], False, last,
                   sem=sT2)
            uslot = tc % 2
            wv = W((sV, sV.c)) if tc >= 2 else W((sV, sV.c), (sS, sS.c))
            dma("Y", utl[:, uslot, :], ub[128 * tc:128 * tc + 128, :], sLD, waits=wv, fence=2)
            op("S", "activation", sem=sS, waits=W((sLD, sLD.c), (sV, sV.c)),
               out=asm1[:, :], in_=utl[:, uslot, :], func=AF.Identity,
               bias=s_mb[:, tc:tc + 1], scale=s_r1[:, tc:tc + 1])
            sfence()
            op("V", "tensor_tensor", waits=W((sS, sS.c)), out=asm1[:, :], in0=asm1[:, :],
               in1=h0s[:, :], op=ALU.mult)
            op("V", "tensor_tensor", waits=W((sT2, sT2.c)), out=asm2[:, 0:512],
               in0=psq[:, 0:512], in1=asm1[:, 0:512], op=ALU.add)
            op("V", "tensor_tensor", sem=sV, out=asm2[:, 512:1024],
               in0=psq[:, 512:1024], in1=asm1[:, 512:1024], op=ALU.add)
            op("V", "tensor_reduce", out=s_sum[:, 0:1], in_=asm2[:, :], axis=AXL.X, op=ALU.add)
            op("V", "tensor_tensor", out=asm1[:, :], in0=asm2[:, :], in1=asm2[:, :], op=ALU.mult)
            op("V", "tensor_reduce", sem=sV, out=s_sq[:, 0:1], in_=asm1[:, :], axis=AXL.X, op=ALU.add)
            ln_finish(s_sum, s_sq, s_m, s_var, s_sd, s_r2, s_mb2,
                      vwaits=W((sV, sV.c)))
            op("S", "activation", sem=sS, waits=W((sV, sV.c)), out=asm1[:, :], in_=asm2[:, :],
               func=AF.Identity, bias=s_mb2[:, 0:1], scale=s_r2[:, 0:1])
            sfence()
            op("V", "tensor_tensor", sem=sV, waits=W((sS, sS.c), (sGD, sGD.c)),
               out=x1t[:, uslot, :], in0=utl[:, uslot, :], in1=asm1[:, :], op=ALU.add)
            dma("G", x1d[L * b + 128 * tc:L * b + 128 * tc + 128, :], x1t[:, uslot, :], sGD,
                waits=W((sV, sV.c)))

    if debug:
        dma("G", d_x1, x1d, sGD)
    barrier("Y", W((sGD, sGD.c)))
    conv.close()

    ffn = contextlib.ExitStack()
    def fbuf(name, shape, dt=F32):
        return ffn.enter_context(nc.sbuf_tensor(name, list(shape), dt))

    w2t = fbuf("w2t", [128, 8, 4096], BF16)
    whi = fbuf("whi", [128, 32, 8, 128], BF16)
    gbuf = fbuf("gbuf", [128, 4, 256], BF16)
    x3t = fbuf("x3t", [128, 8, 256], BF16)
    xh16 = fbuf("xh16", [128, 1024], BF16)
    x1l = fbuf("x1l", [128, 4, 1024])
    ybuf = fbuf("ybuf", [128, 2, 1024])
    fout = fbuf("fout", [128, 1024])
    fstat = fbuf("fstat", [128, 8, 64])
    fscr = fbuf("fscr", [128, 1024])

    f_sum = fstat.ap()[:, 0, :]
    f_sq = fstat.ap()[:, 1, :]
    f_m = fstat.ap()[:, 2, :]
    f_var = fstat.ap()[:, 3, :]
    f_sd = fstat.ap()[:, 4, :]
    f_r = fstat.ap()[:, 5, :]
    f_mb = fstat.ap()[:, 6, :]

    for dc in range(8):
        dma("Y", w2t[:, dc, :], w2ag[128 * dc:128 * dc + 128, :], sLD,
            waits=(W((sG, AG_ALL)) if dc == 0 else []))
    for jc in range(32):
        dma("Y", whi[:, jc, :, :],
            whag.rearrange("(r c p) o -> c p r o", p=128, c=32)[jc], sLD)
    WLOAD = sLD.c
    barrier("T", W((sLD, WLOAD)))

    upslots = [ps_a.ap()[:, 0:256], ps_a.ap()[:, 512:768]]
    dnslots = [ps_b.ap()[:, 0:512], ps_b.ap()[:, 512:1024],
               ps_c.ap()[:, 0:512], ps_c.ap()[:, 512:1024]]

    gelu_done = {}
    epi_v_done = 0
    out_dma_done = 0

    for g in range(NG := 32):
        gslot = g % 2
        for i in range(2):
            wv = W((sV, sV.c)) if g >= 2 else []
            dma("Y", x1l[:, 2 * gslot + i, :], x1d[256 * g + 128 * i:256 * g + 128 * i + 128, :],
                sLD, waits=wv, fence=2)
        for i in range(2):
            xt = x1l[:, 2 * gslot + i, :]
            op("V", "tensor_reduce", waits=W((sLD, sLD.c)),
               out=f_sum[:, 0:1], in_=xt, axis=AXL.X, op=ALU.add)
            op("V", "tensor_tensor", out=fscr[:, :], in0=xt, in1=xt, op=ALU.mult)
            op("V", "tensor_reduce", sem=sV, out=f_sq[:, 0:1], in_=fscr[:, :], axis=AXL.X, op=ALU.add)
            ln_finish(f_sum, f_sq, f_m, f_var, f_sd, f_r, f_mb,
                      vwaits=W((sV, sV.c)))
            op("S", "activation", sem=sS, waits=W((sV, sV.c)), out=xh16[:, :], in_=xt,
               func=AF.Identity, bias=f_mb[:, 0:1], scale=f_r[:, 0:1])
            sfence()
            for dc in range(8):
                tw = W((sV, sV.c)) + (W((sS, sS.c)) if dc == 0 else [])
                tr(ps_t[:, 0:128], xh16[:, 128 * dc:128 * dc + 128], sem=sT, waits=tw)
                op("V", "tensor_copy", sem=sV, waits=W((sT, sT.c)),
                   out=x3t[:, dc, 128 * i:128 * i + 128], in_=ps_t[:, 0:128])
        X3_V = sV.c
        for jc in range(33):
            if jc < 32:
                qs = upslots[jc % 2]
                mw = []
                if jc == 0:
                    mw = W((sV, X3_V))
                if jc >= 2:
                    mw = mw + W((sS, gelu_done[(g, jc - 2)]))
                for dc in range(8):
                    mm(qs, w2t[:, dc, 128 * jc:128 * jc + 128], x3t[:, dc, :],
                       dc == 0, dc == 7, sem=(sT if dc == 7 else None),
                       waits=(mw if dc == 0 else []))
                op("S", "activation", waits=W((sT, sT.c)),
                   out=gbuf[:, jc % 4, :], in_=qs, func=AF.Gelu,
                   bias=biasp[:, jc:jc + 1], scale=1.0)
                sfence()
                gelu_done[(g, jc)] = sS.c
            if jc >= 1:
                jd = jc - 1
                mw = W((sS, gelu_done[(g, jd)]))
                if jd == 0 and g >= 1:
                    mw += W((sV, epi_v_done))
                for k4 in range(4):
                    ts2 = k4 // 2
                    oh = k4 % 2
                    mm(dnslots[k4], gbuf[:, jd % 4, 128 * ts2:128 * ts2 + 128],
                       whi.ap()[:, jd, 4 * oh:4 * oh + 4, :], jd == 0, jd == 31,
                       sem=(sT2 if (jd == 31 and k4 == 3) else None),
                       waits=(mw if k4 == 0 else []))
        for ts2 in range(2):
            wv = W((sT2, sT2.c)) if ts2 == 0 else []
            op("V", "tensor_tensor", waits=wv, out=ybuf[:, ts2, 0:512],
               in0=dnslots[2 * ts2], in1=bias3[:, 0:512], op=ALU.add)
            op("V", "tensor_tensor", sem=sV, out=ybuf[:, ts2, 512:1024],
               in0=dnslots[2 * ts2 + 1], in1=bias3[:, 512:1024], op=ALU.add)
        epi_v_done = sV.c
        for ts2 in range(2):
            yt = ybuf[:, ts2, :]
            op("V", "tensor_reduce", out=f_sum[:, 0:1], in_=yt, axis=AXL.X, op=ALU.add)
            op("V", "tensor_tensor", out=fscr[:, :], in0=yt, in1=yt, op=ALU.mult)
            op("V", "tensor_reduce", sem=sV, out=f_sq[:, 0:1], in_=fscr[:, :], axis=AXL.X, op=ALU.add)
            ln_finish(f_sum, f_sq, f_m, f_var, f_sd, f_r, f_mb,
                      vwaits=W((sV, sV.c)))
            sw2 = W((sV, sV.c)) + (W((sGD, out_dma_done)) if (g > 0 or ts2 > 0) else [])
            op("S", "activation", sem=sS, waits=sw2, out=fout[:, :], in_=yt,
               func=AF.Identity, bias=f_mb[:, 0:1], scale=f_r[:, 0:1])
            sfence()
            op("V", "tensor_tensor", sem=sV, waits=W((sS, sS.c)),
               out=fout[:, :], in0=fout[:, :], in1=x1l[:, 2 * gslot + ts2, :], op=ALU.add)
            tok0 = 256 * g + 128 * ts2
            dma("G", out[tok0 // L][tok0 % L:tok0 % L + 128, :], fout[:, :], sGD,
                waits=W((sV, sV.c)), fence=2)
            out_dma_done = sGD.c

    ffn.close()
    barrier("G", W((sGD, sGD.c), (sLD, sLD.c)))

    B.check_deadlock()

    with nc.Block() as block:
        @block.tensor
        def _(e):
            for k, fn, _, _ in B.ops:
                if k == "T":
                    fn(e)
        @block.vector
        def _(e):
            for k, fn, _, _ in B.ops:
                if k == "V":
                    fn(e)
        @block.scalar
        def _(e):
            for k, fn, _, _ in B.ops:
                if k == "S":
                    fn(e)
        @block.gpsimd
        def _(e):
            for k, fn, _, _ in B.ops:
                if k == "G":
                    fn(e)
        @block.sync
        def _(e):
            for k, fn, _, _ in B.ops:
                if k == "Y":
                    fn(e)

    stack.close()
    return nc


def kernel(L0_attn_u, L0_attn_ab, L0_attn_h_0, L0_ffn_W_exp, L0_ffn_b_exp,
           L0_ffn_up_weights, L0_ffn_up_bias, L0_ffn_down_weights, L0_ffn_W_con,
           L0_ffn_b_con, L0_ffn_down_bias):
    T = _make_tables()
    BT = _make_bfly(np.asarray(L0_ffn_up_weights, np.float32),
                    np.asarray(L0_ffn_down_weights, np.float32))

    u = np.ascontiguousarray(np.asarray(L0_attn_u, np.float32))
    ab = np.asarray(L0_attn_ab, np.float32)
    h0 = np.asarray(L0_attn_h_0, np.float32)
    W_exp = np.asarray(L0_ffn_W_exp, np.float32)
    W_con = np.asarray(L0_ffn_W_con, np.float32)
    b_exp = np.asarray(L0_ffn_b_exp, np.float32)
    up_b = np.asarray(L0_ffn_up_bias, np.float32)
    b_con = np.asarray(L0_ffn_b_con, np.float32)
    dn_b = np.asarray(L0_ffn_down_bias, np.float32)

    def fm(v):
        return np.ascontiguousarray(v.reshape(32, 128).T)

    cvec = np.zeros(8, np.float32)
    cvec[C_EPS] = EPS
    cvec[C_INV1024] = 1.0 / 1024.0
    cvec[C_NEG1] = -1.0
    cvec[C_ZERO] = 0.0

    shared = dict(
        T8re=np.ascontiguousarray(T["T8re"]), T8im=np.ascontiguousarray(T["T8im"]),
        iD8re=np.ascontiguousarray(T["iD8re"]), iD8im=np.ascontiguousarray(T["iD8im"]),
        CS4=np.ascontiguousarray(T["CS4"]),
        iCSre=np.ascontiguousarray(T["iCSre"]), iCSim=np.ascontiguousarray(T["iCSim"]),
        wf4=np.ascontiguousarray(T["wf4"]),
        h0rep=np.ascontiguousarray(np.tile(h0[None, :], (128, 1))),
        bconrep=np.ascontiguousarray(np.tile(b_con[None, :], (128, 1))),
        dbrep=np.ascontiguousarray(np.tile(dn_b[None, :], (128, 1))),
        upb_fm=fm(up_b), bexp_fm=fm(b_exp),
        bflowu=np.ascontiguousarray(BT["up_low"]),
        bflowd=np.ascontiguousarray(BT["dn_low"]),
        bfhighu=np.ascontiguousarray(BT["up_high"]),
        bfhighd=np.ascontiguousarray(BT["dn_high"]),
        ident=np.eye(128, dtype=np.float32).astype(bfdt),
        fsrc=np.zeros((128, 4), np.float32),
        consts=np.ascontiguousarray(np.tile(cvec[None, :], (128, 1))),
    )
    in_maps = []
    for c in range(NCORE):
        aT = np.concatenate([ab[128 * c:128 * c + 128].T, np.ones((1, 128), np.float32)], 0)
        bT = np.concatenate([ab[1024 + 128 * c:1024 + 128 * c + 128].T,
                             np.zeros((1, 128), np.float32)], 0)
        m = dict(shared)
        m["u4"] = np.ascontiguousarray(u[4 * c:4 * c + 4])
        m["aTx"] = np.ascontiguousarray(aT)
        m["bTx"] = np.ascontiguousarray(bT)
        m["Wexp_sl"] = np.ascontiguousarray(W_exp[128 * c:128 * c + 128])
        m["Wcon_sl"] = np.ascontiguousarray(W_con[:, 128 * c:128 * c + 128])
        in_maps.append(m)

    nc = build_program()
    res = run_bass_kernel_spmd(nc, in_maps, core_ids=list(range(NCORE)))
    outs = [np.asarray(res.results[c]["out"], np.float32) for c in range(NCORE)]
    return np.concatenate(outs, axis=0).reshape(32, L, 1024)


if __name__ == "__main__":
    nc = build_program()
    print("build ok; instructions:",
          sum(1 for _ in ()) or "n/a")
